# version 15 (final)
# speedup vs baseline: 1.1048x; 1.0160x over previous
import numpy as np
import contextlib
import concourse.bass as bass
import concourse.mybir as mybir
from concourse.bass_utils import run_bass_kernel_spmd

F32 = mybir.dt.float32
BF16 = mybir.dt.bfloat16
I32 = mybir.dt.int32
AF = mybir.ActivationFunctionType
ALU = mybir.AluOpType

P = 128
T = 2048
D = 1024
EC = D // P
NCH = 4
CH = T // NCH
NST = 16
RK = 64
DF = 4096
FC = DF // P
FH = FC // 2
NCORES = 8
TS = T // NCORES
H = 16
HD = 64
EPS = 1e-5

STAGE = 99
DEV_TAPS = ()
SELF_WAITS = True
SCAN_NC = EC
SCAN_NN = NST
SCAN_PARTS = 31


_COMPUTE_OPS = {
    "memset", "memzero", "tensor_copy", "tensor_tensor", "tensor_add", "tensor_sub",
    "tensor_mul", "tensor_max", "tensor_scalar", "tensor_single_scalar",
    "tensor_scalar_add", "tensor_scalar_sub", "tensor_scalar_mul", "tensor_scalar_max",
    "tensor_scalar_min", "tensor_relu", "tensor_reduce", "reduce_max", "reduce_sum",
    "scalar_tensor_tensor", "tensor_tensor_scan", "reciprocal", "select",
    "copy_predicated", "transpose", "bn_stats", "bn_aggr", "pool", "pool_avg",
    "pool_max", "stream_shuffle", "activation", "square", "sqrt", "copy", "add",
    "mul", "sign", "iota",
}


class EngWrap:

    def __init__(self, eng, ctr):
        self._e = eng
        self._c = ctr

    def __getattr__(self, name):
        f = getattr(self._e, name)
        if name not in _COMPUTE_OPS:
            return f
        ctr = self._c
        eng = self._e

        def g(*a, **k):
            if SELF_WAITS:
                eng.wait_ge(ctr.h, max(0, ctr.v - (_LAG[0] - 1)))
            r = f(*a, **k)
            ctr.inc(r, 1)
            return r
        return g


class Ctr:
    def __init__(self, stack, nc, name):
        self.h = stack.enter_context(nc.semaphore(name))
        self.v = 0

    def inc(self, instr, k=1):
        instr.then_inc(self.h, k)
        self.v += k
        return self.v

    def wait(self, eng, v=None):
        eng.wait_ge(self.h, self.v if v is None else v)


_uid = [0]
_LAG = [1]


def _nm(pfx="t"):
    _uid[0] += 1
    import sys
    return f"{pfx}{_uid[0]}_L{sys._getframe(1).f_lineno}"


def build(taps=()):
    _uid[0] = 0
    nc = bass.Bass("TRN2", target_bir_lowering=False)

    x_t = nc.dram_tensor("x_t", [D, T], F32, kind="ExternalInput").ap()
    rank_in = nc.dram_tensor("rank_in", [1, 1], I32, kind="ExternalInput").ap()

    def win(name, shape):
        return nc.dram_tensor(name, shape, F32, kind="ExternalInput").ap()

    w_in = win("w_in", [P, EC * 2 * D])
    w_xproj = win("w_xproj", [P, EC * 96])
    w_dt = win("w_dt", [RK, EC * P])
    w_outp = win("w_outp", [P, EC * D])
    w_ffn1 = win("w_ffn1", [P, FC * EC * P])
    w_ffn2 = win("w_ffn2", [P, 2 * EC * FH * P])
    w_att = win("w_att", [HD, H * HD])
    w_wv = win("w_wv", [P, EC * D])
    w_outw = win("w_outw", [P, EC * D])
    w_lstm = win("w_lstm", [P, 8 * EC * D])
    wv_b_row = win("wv_b_row", [1, D])
    sm_names = ["ln1_g", "ln1_b", "ln2_g", "ln2_b", "rms_w", "conv_b", "dt_b",
                "d_par", "b2", "lbf", "lbi", "lbc", "lbo", "in_c", "out_b"]
    sm_in = {n: win("sm_" + n, [P, EC]) for n in sm_names}
    b1_in = win("sm_b1", [P, FC])
    conv_w_in = win("conv_w_in", [P, EC * 4])
    sel_in = win("sel_in", [32, 32 * P])
    a_log_in = win("a_log_in", [P, EC * NST])

    out_t = nc.dram_tensor("out_t", [D, T], F32, kind="ExternalOutput").ap()
    tap_ap = {}
    for nm, dt in taps:
        tap_ap[nm] = nc.dram_tensor(nm, [D, T], dt, kind="ExternalOutput").ap()

    vh_dram = nc.dram_tensor("vh_dram", [P, EC * T], F32)
    z_dram = nc.dram_tensor("z_dram", [P, EC * T], BF16)
    h2_dram = nc.dram_tensor("h2_dram", [P, EC * T], BF16)
    lstm_bf = nc.dram_tensor("lstm_bf", [P, 8 * EC * D], BF16)
    ag1_in = nc.dram_tensor("ag1_in", [D, T], BF16)
    ag1_out = nc.dram_tensor("ag1_out", [NCORES * D, T], BF16)
    ag2_in = nc.dram_tensor("ag2_in", [D, T], BF16)
    ag2_out = nc.dram_tensor("ag2_out", [NCORES * D, T], BF16)

    ST = contextlib.ExitStack()
    with ST:
        E = ST.enter_context

        ld = Ctr(ST, nc, "ld")
        gd = Ctr(ST, nc, "gd")
        sd = Ctr(ST, nc, "sd")
        vs = Ctr(ST, nc, "vs")
        ss = Ctr(ST, nc, "ss")
        ts = Ctr(ST, nc, "ts")
        gs = Ctr(ST, nc, "gs")
        ccs = [Ctr(ST, nc, f"cc{i}") for i in range(4)]
        bgi = [Ctr(ST, nc, "bgi0"), Ctr(ST, nc, "bgi1")]
        bgo = [Ctr(ST, nc, "bgo0"), Ctr(ST, nc, "bgo1")]
        zss = [Ctr(ST, nc, "zs0"), Ctr(ST, nc, "zs1")]
        zcs = Ctr(ST, nc, "zcs")
        hss = [Ctr(ST, nc, "hs0"), Ctr(ST, nc, "hs1")]
        wcs = [Ctr(ST, nc, "wc0"), Ctr(ST, nc, "wc1")]
        sds = [Ctr(ST, nc, "sd0"), Ctr(ST, nc, "sd1")]
        xfs = [Ctr(ST, nc, "xf0"), Ctr(ST, nc, "xf1")]
        xrs = [Ctr(ST, nc, "xr0"), Ctr(ST, nc, "xr1")]
        wos = Ctr(ST, nc, "wos")
        w1g = Ctr(ST, nc, "w1g")
        w2g = Ctr(ST, nc, "w2g")
        xrg = Ctr(ST, nc, "xrg")

        V_ = EngWrap(nc.vector, vs)
        S_ = EngWrap(nc.scalar, ss)
        G_ = EngWrap(nc.gpsimd, gs)

        def sbr(name, shape, dt):
            return E(nc.sbuf_tensor(name, shape, dt, side="right"))

        sm = {n: sbr("smv_" + n, [P, EC], F32) for n in sm_names}
        b1_sb = sbr("b1_sb", [P, FC], F32)
        conv_w_sb = sbr("conv_w_sb", [P, EC, 4], F32)
        a_sb = sbr("a_sb", [P, EC, NST], F32)
        ones_row = sbr("ones_row", [1, P], F32)
        ones_row_bf = sbr("ones_row_bf", [1, P], BF16)
        ones_col_bf = sbr("ones_col_bf", [P, 1], BF16)
        rank_sb = sbr("rank_sb", [1, 1], I32)

        const_sb = sbr("const_sb", [P, 8], F32)
        _cvals = [EPS, 1.0, -1.0, 0.125, 1.0 / D, 0.0, -0.5, 0.5]
        for _i, _v in enumerate(_cvals):
            nc.const_aps.register(F32, _v, const_sb.ap()[:, _i:_i + 1])

        for n in sm_names:
            ld.inc(nc.sync.dma_start(out=sm[n][:, :], in_=sm_in[n]), 16)
        ld.inc(nc.sync.dma_start(out=b1_sb[:, :], in_=b1_in), 16)
        ld.inc(nc.sync.dma_start(out=conv_w_sb[:, :, :],
                                 in_=conv_w_in.rearrange("p (c k) -> p c k", k=4)), 16)
        ld.inc(nc.sync.dma_start(out=a_sb[:, :, :],
                                 in_=a_log_in.rearrange("p (c n) -> p c n", n=NST)), 16)
        ld.inc(nc.sync.dma_start(out=rank_sb[:, :], in_=rank_in), 16)
        with contextlib.ExitStack() as e0:
            for _i, _v in enumerate(_cvals):
                V_.memset(const_sb[:, _i:_i + 1], _v)
            ld.wait(nc.vector)
            V_.memset(ones_row[:, :], 1.0)
            V_.memset(ones_row_bf[:, :], 1.0)
            V_.memset(ones_col_bf[:, :], 1.0)
            ld.wait(nc.scalar)
            vs.wait(nc.scalar)
            S_.activation(a_sb[:, :, :], a_sb[:, :, :], AF.Exp)
            ss.wait(nc.vector)
            V_.tensor_scalar_mul(a_sb[:, :, :], a_sb[:, :, :], -1.0)
            vs.wait(nc.sync)
            nc.all_engine_barrier()

        rank_reg_g = nc.sync.register("rank_reg")
        rank_reg = rank_reg_g.__enter__()
        off_reg_g = nc.sync.register("off_reg")
        off_reg = off_reg_g.__enter__()
        nc.sync.reg_load(rank_reg, rank_sb[0:1, 0:1])
        nc.sync.reg_alu(off_reg, rank_reg, TS, ALU.mult)
        off_ts = nc.sync.snap(off_reg)

        BGW = 512
        e_bg = contextlib.ExitStack()
        bg_f = e_bg.enter_context(nc.sbuf_tensor("bg_f", [P, 2, BGW], F32))
        bg_b = e_bg.enter_context(nc.sbuf_tensor("bg_b", [P, 2, BGW], BF16))
        wlv = w_lstm.rearrange("p (q s) -> p q s", s=BGW)
        wld = lstm_bf.ap().rearrange("p (q s) -> p q s", s=BGW)
        n_bg = 8 * EC * D // BGW
        for i in range(min(2, n_bg)):
            bgi[i % 2].inc(nc.gpsimd.dma_start(out=bg_f[:, i % 2, :], in_=wlv[:, i, :]), 16)
        for i in range(n_bg):
            par = i % 2
            nc.gpsimd.wait_ge(bgi[par].h, bgi[par].v)
            if i >= 2:
                nc.gpsimd.wait_ge(bgo[par].h, bgo[par].v)
            G_.tensor_copy(bg_b[:, par, :], bg_f[:, par, :])
            nc.gpsimd.wait_ge(gs.h, gs.v)
            bgo[par].inc(nc.gpsimd.dma_start(out=wld[:, i, :], in_=bg_b[:, par, :]), 16)
            if i + 2 < n_bg:
                bgi[par].inc(nc.gpsimd.dma_start(out=bg_f[:, par, :], in_=wlv[:, i + 2, :]), 16)

        def stat_pass(src_fn, rows, era, square=False, weights=None):
            xe = era.enter_context
            nrows = len(rows)
            weights = weights or [None] * nrows
            ping = xe(nc.sbuf_tensor(_nm(), [P, 2, CH], BF16))
            pss = [xe(nc.psum_tensor(_nm('ps'), [1, CH], F32)) for _ in range(nrows)]
            wcol = []
            for w in weights:
                if w is None:
                    wcol.append(None)
                else:
                    wc = xe(nc.sbuf_tensor(_nm(), [P, EC], BF16))
                    V_.tensor_copy(wc[:, :], w[:, :])
                    wcol.append(wc)
            mm_tok = {}
            for j in range(NCH):
                for c in range(EC):
                    par = (j * EC + c) % 2
                    prev = mm_tok.get(par)
                    eng = nc.scalar if square else nc.vector
                    if prev is not None:
                        eng.wait_ge(ts.h, prev)
                    sl = src_fn(c)[:, j * CH:(j + 1) * CH]
                    if square:
                        nc.scalar.wait_ge(vs.h, vs.v)
                        S_.square(ping[:, par, :], sl)
                        nc.tensor.wait_ge(ss.h, ss.v)
                    else:
                        V_.tensor_copy(ping[:, par, :], sl)
                        nc.tensor.wait_ge(vs.h, vs.v)
                    if c == 0:
                        nc.tensor.wait_ge(vs.h, vs.v)
                    for i in range(nrows):
                        lh = ones_col_bf[:, 0:1] if wcol[i] is None else wcol[i][:, c:c + 1]
                        ts.inc(nc.tensor.matmul(pss[i][:, :], lh, ping[:, par, :],
                                                start=(c == 0), stop=(c == EC - 1)))
                    mm_tok[par] = ts.v
                ts.wait(nc.vector)
                for i in range(nrows):
                    V_.tensor_copy(rows[i][:, j * CH:(j + 1) * CH], pss[i][:, :])

        def rsqrt_row(dst, src, scale=1.0, bias=EPS):
            vs.wait(nc.scalar)
            S_.activation(dst[:, :], src[:, :], AF.Ln, bias=bias, scale=scale)
            nc.scalar.wait_ge(ss.h, ss.v)
            S_.activation(dst[:, :], dst[:, :], AF.Exp, scale=-0.5)
            ss.wait(nc.vector)

        def soft_barrier():
            for eng, own in [(nc.vector, vs), (nc.scalar, ss), (nc.tensor, ts)]:
                for c2 in (vs, ss, ts):
                    if c2 is not own:
                        eng.wait_ge(c2.h, c2.v)
                eng.wait_ge(ld.h, ld.v)
            for c2 in (vs, ss, ts):
                nc.sync.wait_ge(c2.h, c2.v)

        def gpsimd_join():
            for c2 in (vs, ss, ts):
                nc.gpsimd.wait_ge(c2.h, c2.v)
            nc.gpsimd.wait_ge(ld.h, ld.v)

        def rep_chunk(ps, row_ap, j, bf=False, m=P):
            lhs = (ones_row_bf if bf else ones_row)[:, 0:m]
            ts.inc(nc.tensor.matmul(ps, lhs, row_ap[:, j * CH:(j + 1) * CH],
                                    start=True, stop=True))

        def mu_rstd_rows(src_fn, r_mu, r_rstd, era):
            with contextlib.ExitStack() as se:
                xe2 = se.enter_context
                ping = xe2(nc.sbuf_tensor(_nm(), [P, 2, CH], BF16))
                ping2 = xe2(nc.sbuf_tensor(_nm(), [P, 2, CH], BF16))
                ps_mu = xe2(nc.psum_tensor(_nm('ps'), [1, CH], F32))
                ps_sq = xe2(nc.psum_tensor(_nm('ps'), [1, CH], F32))
                nc.scalar.wait_ge(vs.h, vs.v)
                nc.scalar.wait_ge(ld.h, ld.v)
                mm_tok = {}
                for j in range(NCH):
                    for c in range(EC):
                        par = (j * EC + c) % 2
                        sl = src_fn(c)[:, j * CH:(j + 1) * CH]
                        prev = mm_tok.get(par)
                        if prev is not None:
                            nc.vector.wait_ge(ts.h, prev)
                            nc.scalar.wait_ge(ts.h, prev)
                        V_.tensor_copy(ping[:, par, :], sl)
                        cp_tok = vs.v
                        S_.square(ping2[:, par, :], sl)
                        sq_tok = ss.v
                        nc.tensor.wait_ge(vs.h, cp_tok)
                        nc.tensor.wait_ge(ss.h, sq_tok)
                        ts.inc(nc.tensor.matmul(ps_mu[:, :], ones_col_bf[:, 0:1],
                                                ping[:, par, :],
                                                start=(c == 0), stop=(c == EC - 1)))
                        ts.inc(nc.tensor.matmul(ps_sq[:, :], ones_col_bf[:, 0:1],
                                                ping2[:, par, :],
                                                start=(c == 0), stop=(c == EC - 1)))
                        mm_tok[par] = ts.v
                    ts.wait(nc.vector)
                    V_.tensor_copy(r_mu[:, j * CH:(j + 1) * CH], ps_mu[:, :])
                    V_.tensor_copy(r_rstd[:, j * CH:(j + 1) * CH], ps_sq[:, :])
            V_.tensor_scalar_mul(r_mu[:, :], r_mu[:, :], 1.0 / D)
            tmp = era.enter_context(nc.sbuf_tensor(_nm(), [1, T], F32))
            vs.wait(nc.scalar)
            S_.square(tmp[:, :], r_mu[:, :])
            ss.wait(nc.vector)
            V_.scalar_tensor_tensor(r_rstd[:, :], r_rstd[:, :], 1.0 / D,
                                                  tmp[:, :], ALU.mult, ALU.subtract)
            rsqrt_row(r_rstd, r_rstd)

        def ln_apply(src_fn, dst_fn, r_mu, r_rstd, era, g=None, b=None):
            ps_mu = era.enter_context(nc.psum_tensor(_nm('ps'), [P, CH], F32))
            ps_rs = era.enter_context(nc.psum_tensor(_nm('ps'), [P, CH], F32))
            tmp = era.enter_context(nc.sbuf_tensor(_nm(), [P, 2, CH], F32))
            for j in range(NCH):
                nc.tensor.wait_ge(vs.h, vs.v)
                nc.tensor.wait_ge(ss.h, ss.v)
                rep_chunk(ps_mu[:, :], r_mu, j)
                rep_chunk(ps_rs[:, :], r_rstd, j)
                ts.wait(nc.vector)
                sl = slice(j * CH, (j + 1) * CH)
                _LAG[0] = 2
                for c0 in range(0, EC, 2):
                    for c in (c0, c0 + 1):
                        V_.tensor_sub(tmp[:, c % 2, :], src_fn(c)[:, sl], ps_mu[:, :])
                    if g is None:
                        for c in (c0, c0 + 1):
                            V_.tensor_mul(dst_fn(c)[:, sl], tmp[:, c % 2, :], ps_rs[:, :])
                    else:
                        for c in (c0, c0 + 1):
                            V_.tensor_mul(tmp[:, c % 2, :], tmp[:, c % 2, :], ps_rs[:, :])
                        for c in (c0, c0 + 1):
                            V_.tensor_scalar(dst_fn(c)[:, sl], tmp[:, c % 2, :],
                                                           g[:, c:c + 1], b[:, c:c + 1],
                                                           ALU.mult, ALU.add)
                _LAG[0] = 1

        def stream_cast(dram_view_fn, dst, n_chunks, era, tokens_out=None, split=1):
            xe = era.enter_context
            width = dst.shape[-1]
            if split > 1:
                width //= split
                dst = dst.ap().rearrange("p c (s w) -> p (c s) w", s=split)
                n_chunks *= split
                inner = dram_view_fn
                dram_view_fn = lambda c: inner(c // split)[:, (c % split) * width:(c % split + 1) * width]
            stg = xe(nc.sbuf_tensor(_nm(), [P, 2, width], F32))
            stoks = []
            for c in range(n_chunks):
                if c >= 2:
                    nc.sync.wait_ge(ss.h, stoks[c - 2])
                wcs[c % 2].inc(nc.sync.dma_start(out=stg[:, c % 2, :], in_=dram_view_fn(c)), 16)
                nc.scalar.wait_ge(wcs[c % 2].h, wcs[c % 2].v)
                S_.copy(dst[:, c, :], stg[:, c % 2, :])
                stoks.append(ss.v)
            if tokens_out is not None:
                tokens_out.extend(stoks)
            ss.wait(nc.sync)
            return ss.v

        def tap(nm, src_view):
            if nm not in tap_ap:
                return
            soft_barrier()
            ld.inc(nc.sync.dma_start(out=tap_ap[nm].rearrange("(c p) t -> p c t", p=P),
                                     in_=src_view), 16)
            ld.wait(nc.sync)
            soft_barrier()

        e_m = ST.enter_context(contextlib.ExitStack())
        uy_bf = e_m.enter_context(nc.sbuf_tensor("uy_bf", [P, EC, T], BF16))

        with contextlib.ExitStack() as era:
            xe = era.enter_context
            xv = xe(nc.sbuf_tensor(_nm(), [P, EC, T], F32))
            ld.inc(nc.sync.dma_start(out=xv[:, :, :],
                                     in_=x_t.rearrange("(c p) t -> p c t", p=P)), 16)
            ld.wait(nc.vector)
            ld.wait(nc.scalar)
            ld.wait(nc.tensor)
            tap("tap_x", xv[:, :, :])
            xvf = lambda c: xv[:, c, :]
            r_mu = xe(nc.sbuf_tensor(_nm(), [1, T], F32))
            r_rstd = xe(nc.sbuf_tensor(_nm(), [1, T], F32))
            mu_rstd_rows(xvf, r_mu, r_rstd, era)
            ln_apply(xvf, xvf, r_mu, r_rstd, era)
            r_mu1 = xe(nc.sbuf_tensor(_nm(), [1, T], F32))
            r_rstd1 = xe(nc.sbuf_tensor(_nm(), [1, T], F32))
            mu_rstd_rows(xvf, r_mu1, r_rstd1, era)
            r_hsq = xe(nc.sbuf_tensor(_nm(), [1, T], F32))
            with contextlib.ExitStack() as e2:
                x2 = e2.enter_context
                ps_mu = x2(nc.psum_tensor(_nm('ps'), [P, CH], F32))
                ps_rs = x2(nc.psum_tensor(_nm('ps'), [P, CH], F32))
                ps_hq = x2(nc.psum_tensor(_nm('ps'), [1, CH], F32))
                hch = x2(nc.sbuf_tensor(_nm(), [P, 2, CH], F32))
                hbf = x2(nc.sbuf_tensor(_nm(), [P, 2, CH], BF16))
                mmtok = {}
                for j in range(NCH):
                    nc.tensor.wait_ge(vs.h, vs.v)
                    nc.tensor.wait_ge(ss.h, ss.v)
                    rep_chunk(ps_mu[:, :], r_mu1, j)
                    rep_chunk(ps_rs[:, :], r_rstd1, j)
                    ts.wait(nc.vector)
                    for c in range(EC):
                        par = c % 2
                        prev = mmtok.get(par)
                        if prev is not None:
                            nc.vector.wait_ge(ts.h, prev)
                        sl = slice(j * CH, (j + 1) * CH)
                        V_.tensor_sub(hch[:, par, :], xv[:, c, sl], ps_mu[:, :])
                        V_.tensor_mul(hch[:, par, :], hch[:, par, :], ps_rs[:, :])
                        V_.tensor_scalar(hch[:, par, :], hch[:, par, :],
                                                       sm["ln1_g"][:, c:c + 1], sm["ln1_b"][:, c:c + 1],
                                                       ALU.mult, ALU.add)
                        vs.wait(nc.scalar)
                        if prev is not None:
                            nc.scalar.wait_ge(ts.h, prev)
                        S_.square(hbf[:, par, :], hch[:, par, :])
                        ss.wait(nc.tensor)
                        ts.inc(nc.tensor.matmul(ps_hq[:, :], ones_col_bf[:, 0:1], hbf[:, par, :],
                                                start=(c == 0), stop=(c == EC - 1)))
                        mmtok[par] = ts.v
                    ts.wait(nc.vector)
                    V_.tensor_copy(r_hsq[:, j * CH:(j + 1) * CH], ps_hq[:, :])
            rsqrt_row(r_hsq, r_hsq, scale=1.0 / D)
            r_s = r_hsq
            with contextlib.ExitStack() as e2:
                x2 = e2.enter_context
                ps_mu = x2(nc.psum_tensor(_nm('ps'), [P, CH], F32))
                ps_rs = x2(nc.psum_tensor(_nm('ps'), [P, CH], F32))
                ps_s = x2(nc.psum_tensor(_nm('ps'), [P, CH], F32))
                hch = x2(nc.sbuf_tensor(_nm(), [P, CH], F32))
                for j in range(NCH):
                    nc.tensor.wait_ge(vs.h, vs.v)
                    nc.tensor.wait_ge(ss.h, ss.v)
                    rep_chunk(ps_mu[:, :], r_mu1, j)
                    rep_chunk(ps_rs[:, :], r_rstd1, j)
                    rep_chunk(ps_s[:, :], r_s, j)
                    ts.wait(nc.vector)
                    for c in range(EC):
                        sl = slice(j * CH, (j + 1) * CH)
                        V_.tensor_sub(hch[:, :], xv[:, c, sl], ps_mu[:, :])
                        V_.tensor_mul(hch[:, :], hch[:, :], ps_rs[:, :])
                        V_.tensor_scalar(hch[:, :], hch[:, :],
                                                       sm["ln1_g"][:, c:c + 1], sm["ln1_b"][:, c:c + 1],
                                                       ALU.mult, ALU.add)
                        V_.scalar_tensor_tensor(uy_bf[:, c, sl], hch[:, :],
                                                              sm["rms_w"][:, c:c + 1], ps_s[:, :],
                                                              ALU.mult, ALU.mult)
                        V_.tensor_add(xv[:, c, sl], xv[:, c, sl], hch[:, :])
            vs.wait(nc.sync)
            ld.inc(nc.sync.dma_start(out=vh_dram.ap().rearrange("p (c t) -> p c t", t=T),
                                     in_=xv[:, :, :]), 16)
            ld.wait(nc.sync)
            soft_barrier()
        tap("tap_u", uy_bf[:, :, :])

        e_xc = e_m.enter_context(contextlib.ExitStack())
        xc = e_xc.enter_context(nc.sbuf_tensor("xc", [P, EC, T + 8], F32))

        if STAGE >= 2:
            with contextlib.ExitStack() as era:
                xe = era.enter_context
                w_bf = xe(nc.sbuf_tensor(_nm(), [P, EC, 2 * D], BF16))
                xi_t = xe(nc.sbuf_tensor(_nm(), [P, 2, T + 8], F32))
                zp = xe(nc.sbuf_tensor(_nm(), [P, 2, T], BF16))
                ps_a = xe(nc.psum_tensor(_nm('ps'), [P, CH], F32))
                ps_b = xe(nc.psum_tensor(_nm('ps'), [P, CH], F32))
                ctoks = []
                cast_done = 0
                with contextlib.ExitStack() as cera:
                    cast_done = stream_cast(
                        lambda c: w_in.rearrange("p (c f) -> p c f", f=2 * D)[:, c, :],
                        w_bf, EC, cera, tokens_out=ctoks)
                    V_.memset(xi_t[:, :, 0:4], 0.0)
                    for c in range(EC):
                        V_.memset(xc[:, c, 0:4], 0.0)
                    evtok = {}
                    for fo in range(2 * EC):
                        is_z = fo >= EC
                        for j in range(NCH):
                            ps = ps_a if (fo * NCH + j) % 2 == 0 else ps_b
                            nc.tensor.wait_ge(ss.h, ctoks[-1])
                            w = evtok.get(id(ps))
                            if w is not None:
                                nc.tensor.wait_ge(w[0], w[1])
                            for c in range(EC):
                                ts.inc(nc.tensor.matmul(ps[:, :], w_bf[:, c, fo * P:(fo + 1) * P],
                                                        uy_bf[:, c, j * CH:(j + 1) * CH],
                                                        start=(c == 0), stop=(c == EC - 1)))
                            if is_z:
                                ts.wait(nc.scalar)
                                zpar = (fo - EC) % 2
                                nc.scalar.wait_ge(zss[zpar].h, zss[zpar].v)
                                S_.activation(zp[:, zpar, j * CH:(j + 1) * CH],
                                                            ps[:, :], AF.Silu)
                                evtok[id(ps)] = (ss.h, ss.v)
                            else:
                                ts.wait(nc.vector)
                                V_.tensor_copy(
                                    xi_t[:, fo % 2, 4 + j * CH:4 + (j + 1) * CH], ps[:, :])
                                evtok[id(ps)] = (vs.h, vs.v)
                        if is_z:
                            c = fo - EC
                            ss.wait(nc.sync)
                            zss[c % 2].inc(nc.sync.dma_start(
                                out=z_dram.ap().rearrange("p (c t) -> p c t", t=T)[:, c, :],
                                in_=zp[:, c % 2, :]), 16)
                        else:
                            c = fo
                            acc = xc[:, c, 4:4 + T]
                            V_.tensor_scalar_mul(acc, xi_t[:, c % 2, 1:1 + T],
                                                               conv_w_sb[:, c, 0:1])
                            for k in range(1, 4):
                                V_.scalar_tensor_tensor(
                                    acc, xi_t[:, c % 2, 1 + k:1 + k + T],
                                    conv_w_sb[:, c, k:k + 1], acc, ALU.mult, ALU.add)
                            vs.wait(nc.scalar)
                            S_.activation(acc, acc, AF.Silu,
                                                        bias=sm["conv_b"][:, c:c + 1])
                    ld.wait(nc.sync)
                    nc.sync.wait_ge(zss[0].h, zss[0].v)
                    nc.sync.wait_ge(zss[1].h, zss[1].v)
                    ss.wait(nc.scalar)
                soft_barrier()
            tap("tap_xc", xc[:, :, 4:4 + T])

        if STAGE >= 3:
            with contextlib.ExitStack() as era:
                xe = era.enter_context
                dbc = xe(nc.sbuf_tensor(_nm(), [96, T], F32))
                bc_bf = xe(nc.sbuf_tensor(_nm(), [32, T], BF16))
                dt_sb = xe(nc.sbuf_tensor(_nm(), [RK, EC, P], F32))
                ld.inc(nc.sync.dma_start(out=dt_sb[:, :, :],
                                         in_=w_dt.rearrange("r (c f) -> r c f", f=P)), 16)
                with contextlib.ExitStack() as dbc_era:
                    xproj_sb = dbc_era.enter_context(nc.sbuf_tensor(_nm(), [P, EC, 96], F32))
                    ld.inc(nc.sync.dma_start(out=xproj_sb[:, :, :],
                                             in_=w_xproj.rearrange("p (c f) -> p c f", f=96)), 16)
                    ps96 = dbc_era.enter_context(nc.psum_tensor(_nm('ps'), [96, CH], F32))
                    ld.wait(nc.tensor)
                    for j in range(NCH):
                        nc.tensor.wait_ge(vs.h, vs.v)
                        for c in range(EC):
                            ts.inc(nc.tensor.matmul(ps96[:, :], xproj_sb[:, c, :],
                                                    xc[:, c, 4 + j * CH:4 + (j + 1) * CH],
                                                    start=(c == 0), stop=(c == EC - 1)))
                        ts.wait(nc.vector)
                        V_.tensor_copy(dbc[:, j * CH:(j + 1) * CH], ps96[:, :])
                    V_.tensor_copy(bc_bf[:, :], dbc[64:96, :])
                    soft_barrier()
                if "tap_dbc" in tap_ap:
                    ld.inc(nc.sync.dma_start(out=tap_ap["tap_dbc"][0:96, :], in_=dbc[:, :]), 16)
                    ld.wait(nc.sync)
                    soft_barrier()

                if STAGE >= 4:
                    sel_bf = xe(nc.sbuf_tensor(_nm(), [32, 32, P], BF16))
                    with contextlib.ExitStack() as sel_era:
                        sel_f = sel_era.enter_context(nc.sbuf_tensor(_nm(), [32, 32 * P], F32))
                        ld.inc(nc.sync.dma_start(out=sel_f[:, :], in_=sel_in), 16)
                        ld.wait(nc.vector)
                        V_.tensor_copy(sel_bf.ap().rearrange("k n m -> k (n m)"), sel_f[:, :])
                        vs.wait(nc.sync)
                    delta = xe(nc.sbuf_tensor(_nm(), [P, T], F32))
                    dmx = xe(nc.sbuf_tensor(_nm(), [P, T], F32))
                    da = xe(nc.sbuf_tensor(_nm(), [P, 2, T], F32))
                    bx = xe(nc.sbuf_tensor(_nm(), [P, T], F32))
                    hsc = xe(nc.sbuf_tensor(_nm(), [P, 2, T], F32))
                    yacc = xe(nc.sbuf_tensor(_nm(), [P, T], F32))
                    zc = xe(nc.sbuf_tensor(_nm(), [P, 1, T], BF16))
                    ps_B = xe(nc.psum_tensor(_nm('ps'), [P, T], F32))
                    ps_C = xe(nc.psum_tensor(_nm('ps'), [P, T], F32))
                    psBv = ps_B.ap().rearrange("p (j t) -> p j t", j=NCH)
                    psCv = ps_C.ap().rearrange("p (j t) -> p j t", j=NCH)
                    z_v = z_dram.ap().rearrange("p (c t) -> p c t", t=T)

                    da_tok = [0, 0]
                    g_tok = {}
                    bx_tok = 0
                    yc_tok = 0
                    brep_tok = 0
                    crep_tok = 0
                    bxc_tok = 0
                    bxg_scan_tok = 0
                    gbx_tok = 0
                    z_tok = {}
                    for c in range(SCAN_NC):
                        if SCAN_PARTS & 1:
                            zprev = z_tok.get(0)
                            if zprev is not None:
                                nc.sync.wait_ge(vs.h, zprev)
                            zcs.inc(nc.sync.dma_start(out=zc[:, 0, :], in_=z_v[:, c, :]), 16)
                        if SCAN_PARTS & 2:
                            for j in range(NCH):
                                nc.tensor.wait_ge(vs.h, vs.v)
                                nc.tensor.wait_ge(ss.h, ss.v)
                                ts.inc(nc.tensor.matmul(ps_B[:, 0:CH], dt_sb[:, c, :],
                                                        dbc[0:RK, j * CH:(j + 1) * CH],
                                                        start=True, stop=True))
                                ts.wait(nc.scalar)
                                ts.wait(nc.vector)
                                nc.scalar.wait_ge(vs.h, vs.v)
                                sl = slice(j * CH, (j + 1) * CH)
                                if SCAN_PARTS & 4:
                                    S_.activation(bx[:, sl], ps_B[:, 0:CH], AF.Abs,
                                                  bias=sm["dt_b"][:, c:c + 1])
                                if SCAN_PARTS & 8:
                                    ss.wait(nc.vector)
                                    V_.tensor_scalar(delta[:, sl], ps_B[:, 0:CH],
                                                     sm["dt_b"][:, c:c + 1], 0.0,
                                                     ALU.add, ALU.max)
                        if SCAN_PARTS & 4:
                            S_.activation(bx[:, :], bx[:, :], AF.Exp, scale=-1.0)
                            S_.activation(bx[:, :], bx[:, :], AF.Ln, bias=1.0)
                        if SCAN_PARTS & 64:
                            ss.wait(nc.vector)
                            V_.tensor_add(delta[:, :], delta[:, :], bx[:, :])
                        if SCAN_PARTS & 16:
                            V_.tensor_mul(dmx[:, :], delta[:, :], xc[:, c, 4:4 + T])
                        vtok_d = vs.v
                        for n in range(SCAN_NN):
                            par = n % 2
                            nc.scalar.wait_ge(vs.h, max(vtok_d, da_tok[par]))
                            S_.activation(da[:, par, :], delta[:, :], AF.Exp,
                                                        scale=a_sb[:, c, n:n + 1])
                            stok_da = ss.v
                            nc.tensor.wait_ge(vs.h, bx_tok)
                            if n == 0:
                                nc.tensor.wait_ge(ss.h, ss.v)
                                nc.tensor.wait_ge(vs.h, vs.v)
                            for j in range(NCH):
                                ts.inc(nc.tensor.matmul(psBv[:, j, :], sel_bf[:, n, :],
                                                        bc_bf[:, j * CH:(j + 1) * CH],
                                                        start=True, stop=True))
                            brep_tok = ts.v
                            nc.tensor.wait_ge(vs.h, yc_tok)
                            for j in range(NCH):
                                ts.inc(nc.tensor.matmul(psCv[:, j, :], sel_bf[:, 16 + n, :],
                                                        bc_bf[:, j * CH:(j + 1) * CH],
                                                        start=True, stop=True))
                            crep_tok = ts.v
                            nc.vector.wait_ge(ts.h, brep_tok)
                            V_.tensor_mul(bx[:, :], dmx[:, :], ps_B[:, :])
                            bx_tok = vs.v
                            nc.vector.wait_ge(ss.h, stok_da)
                            nc.vector.wait_ge(gs.h, g_tok.get(par, 0))
                            V_.tensor_tensor_scan(hsc[:, par, :], da[:, par, :], bx[:, :],
                                                                0.0, ALU.mult, ALU.add)
                            da_tok[par] = vs.v
                            nc.vector.wait_ge(ts.h, crep_tok)
                            V_.tensor_mul(hsc[:, par, :], hsc[:, par, :], ps_C[:, :])
                            yc_tok = vs.v
                            nc.gpsimd.wait_ge(vs.h, yc_tok)
                            if n == 0:
                                nc.gpsimd.wait_ge(vs.h, vs.v)
                            if n == 0:
                                G_.tensor_copy(yacc[:, :], hsc[:, par, :])
                            else:
                                G_.tensor_tensor(yacc[:, :], yacc[:, :], hsc[:, par, :], ALU.add)
                            g_tok[par] = gs.v
                        if SCAN_PARTS & 16:
                            gs.wait(nc.vector)
                            V_.scalar_tensor_tensor(yacc[:, :], xc[:, c, 4:4 + T],
                                                                  sm["d_par"][:, c:c + 1], yacc[:, :],
                                                                  ALU.mult, ALU.add)
                            zcs.wait(nc.vector)
                            V_.tensor_mul(uy_bf[:, c, :], yacc[:, :], zc[:, 0, :])
                            z_tok[0] = vs.v
                    ld.wait(nc.sync)
                    zcs.wait(nc.sync)
                    soft_barrier()
            tap("tap_y", uy_bf[:, :, :])

        e_xc.close()

        sA_g = nc.sbuf_tensor("sA", [P, EC, T], F32, side="right")
        sA = sA_g.__enter__()
        if STAGE < 5:
            V_.memset(sA[:, :, :], 0.0)

        if STAGE >= 5:
            ld.inc(nc.sync.dma_start(out=sA[:, :, :],
                                     in_=vh_dram.ap().rearrange("p (c t) -> p c t", t=T)), 16)
            vh_tok = ld.v
            with contextlib.ExitStack() as era:
                xe = era.enter_context
                w_bf = xe(nc.sbuf_tensor(_nm(), [P, EC, D], BF16))
                ps_a = xe(nc.psum_tensor(_nm('ps'), [P, CH], F32))
                ps_b = xe(nc.psum_tensor(_nm('ps'), [P, CH], F32))
                with contextlib.ExitStack() as cera:
                    ctoks = []
                    stream_cast(lambda c: w_outp.rearrange("p (c f) -> p c f", f=D)[:, c, :],
                                w_bf, EC, cera, tokens_out=ctoks)
                    ld.wait(nc.vector)
                    evtok = {}
                    for do in range(EC):
                        for j in range(NCH):
                            ps = ps_a if (do * NCH + j) % 2 == 0 else ps_b
                            nc.tensor.wait_ge(ss.h, ctoks[-1])
                            w = evtok.get(id(ps))
                            if w is not None:
                                nc.tensor.wait_ge(w[0], w[1])
                            for c in range(EC):
                                ts.inc(nc.tensor.matmul(ps[:, :], w_bf[:, c, do * P:(do + 1) * P],
                                                        uy_bf[:, c, j * CH:(j + 1) * CH],
                                                        start=(c == 0), stop=(c == EC - 1)))
                            ts.wait(nc.vector)
                            sl = slice(j * CH, (j + 1) * CH)
                            V_.tensor_add(sA[:, do, sl], sA[:, do, sl], ps[:, :])
                            evtok[id(ps)] = (vs.h, vs.v)
                soft_barrier()
            tap("tap_v2", sA[:, :, :])

        e_m.close()
        for _eng in (nc.vector, nc.scalar, nc.tensor, nc.sync):
            _eng.wait_ge(bgo[0].h, bgo[0].v)
            _eng.wait_ge(bgo[1].h, bgo[1].v)
        e_bg.close()

        if STAGE >= 6:
            with contextlib.ExitStack() as era:
                xe = era.enter_context
                r_mu = xe(nc.sbuf_tensor(_nm(), [1, T], F32))
                r_rstd = xe(nc.sbuf_tensor(_nm(), [1, T], F32))
                sAf = lambda c: sA[:, c, :]
                mu_rstd_rows(sAf, r_mu, r_rstd, era)
                h2st = xe(nc.sbuf_tensor(_nm(), [P, EC, CH], BF16))
                h2v = h2_dram.ap().rearrange("p (c t) -> p c t", t=T)
                ps_mu = xe(nc.psum_tensor(_nm('ps'), [P, CH], F32))
                ps_rs = xe(nc.psum_tensor(_nm('ps'), [P, CH], F32))
                tmp = xe(nc.sbuf_tensor(_nm(), [P, CH], F32))
                h2tok = 0
                for j in range(NCH):
                    nc.tensor.wait_ge(vs.h, vs.v)
                    nc.tensor.wait_ge(ss.h, ss.v)
                    rep_chunk(ps_mu[:, :], r_mu, j)
                    rep_chunk(ps_rs[:, :], r_rstd, j)
                    ts.wait(nc.vector)
                    nc.vector.wait_ge(ld.h, h2tok)
                    for c in range(EC):
                        sl = slice(j * CH, (j + 1) * CH)
                        V_.tensor_sub(tmp[:, :], sA[:, c, sl], ps_mu[:, :])
                        V_.tensor_mul(tmp[:, :], tmp[:, :], ps_rs[:, :])
                        V_.tensor_scalar(h2st[:, c, :], tmp[:, :],
                                                       sm["ln2_g"][:, c:c + 1], sm["ln2_b"][:, c:c + 1],
                                                       ALU.mult, ALU.add)
                    vs.wait(nc.sync)
                    ld.inc(nc.sync.dma_start(out=h2v[:, :, j * CH:(j + 1) * CH],
                                             in_=h2st[:, :, :]), 16)
                    h2tok = ld.v
                soft_barrier()

            e7 = contextlib.ExitStack()
            with e7:
                xe = e7.enter_context
                e7i = xe(contextlib.ExitStack())
                x7 = e7i.enter_context
                hid = x7(nc.sbuf_tensor(_nm(), [P, FH, T], BF16))
                xg = x7(nc.sbuf_tensor(_nm(), [P, 2, CH], F32))
                h2f = x7(nc.sbuf_tensor(_nm(), [P, EC, T], BF16))
                w1f = x7(nc.sbuf_tensor(_nm(), [P, 2, EC * P], F32))
                w1b = x7(nc.sbuf_tensor(_nm(), [P, 2, EC * P], BF16))
                w2f = x7(nc.sbuf_tensor(_nm(), [P, 2, FH * P], F32))
                w2b = x7(nc.sbuf_tensor(_nm(), [P, 1, FH * P], BF16))
                ps_a = x7(nc.psum_tensor(_nm('ps'), [P, CH], F32))
                ps_b = x7(nc.psum_tensor(_nm('ps'), [P, CH], F32))
                ld.wait(nc.sync)
                ld.inc(nc.sync.dma_start(out=h2f[:, :, :], in_=h2v[:, :, :]), 16)
                h2f_tok = ld.v
                w1v = w_ffn1.rearrange("p (f ck) -> p f ck", f=FC)
                w2v = w_ffn2.rearrange("p (h d kk) -> p h d kk", h=2, d=EC, kk=FH * P)
                w1bv = w1b.ap().rearrange("p q (c k) -> p q c k", k=P)
                w2bv = w2b.ap().rearrange("p q (kc k) -> p q kc k", k=P)
                xv_ = x_t.rearrange("(c p) t -> p c t", p=P)
                w1_mm_tok = {}
                w2_mm_tok = {}
                w2_last_mm = 0
                ev7 = {}
                xtok = {}
                gpsimd_join()
                for half in range(2):
                    for f0h in range(FH):
                        f0 = half * FH + f0h
                        par = f0h % 2
                        prev = w1_mm_tok.get(par)
                        q1 = nc.sync if par == 0 else nc.gpsimd
                        s1 = wcs[0] if par == 0 else w1g
                        if prev is not None:
                            q1.wait_ge(ss.h, prev[0])
                        s1.inc(q1.dma_start(out=w1f[:, par, :], in_=w1v[:, f0, :]), 16)
                        nc.scalar.wait_ge(s1.h, s1.v)
                        if prev is not None:
                            nc.scalar.wait_ge(ts.h, prev[1])
                        S_.copy(w1b[:, par, :], w1f[:, par, :])
                        cast_tok = ss.v
                        nc.tensor.wait_ge(ld.h, ld.v)
                        nc.tensor.wait_ge(ss.h, cast_tok)
                        for j in range(NCH):
                            ps = ps_a if (f0h * NCH + j) % 2 == 0 else ps_b
                            w = ev7.get(id(ps))
                            if w is not None:
                                nc.tensor.wait_ge(w[0], w[1])
                            for c in range(EC):
                                ts.inc(nc.tensor.matmul(ps[:, :], w1bv[:, par, c, :],
                                                        h2f[:, c, j * CH:(j + 1) * CH],
                                                        start=(c == 0), stop=(c == EC - 1)))
                            ts.wait(nc.scalar)
                            S_.activation(hid[:, f0h, j * CH:(j + 1) * CH],
                                                        ps[:, :], AF.Relu,
                                                        bias=b1_sb[:, f0:f0 + 1])
                            ev7[id(ps)] = (ss.h, ss.v)
                        w1_mm_tok[par] = (ss.v, ts.v)
                    for do in range(EC):
                        par = do % 2
                        prev = w2_mm_tok.get(par)
                        q2 = nc.scalar if par == 0 else nc.gpsimd
                        s2 = sds[0] if par == 0 else w2g
                        if prev is not None:
                            q2.wait_ge(ts.h, prev[1])
                        s2.inc(q2.dma_start(out=w2f[:, par, :], in_=w2v[:, half, do, :]), 16)
                        nc.scalar.wait_ge(s2.h, s2.v)
                        nc.scalar.wait_ge(ts.h, w2_last_mm)
                        S_.copy(w2b[:, 0, :], w2f[:, par, :])
                        cast_tok = ss.v
                        nc.tensor.wait_ge(ss.h, cast_tok)
                        for j in range(NCH):
                            ps = ps_a if (do * NCH + j) % 2 == 0 else ps_b
                            if half == 1:
                                parx = j % 2
                                prevx = xtok.get(parx)
                                q3 = nc.sync if parx == 0 else nc.gpsimd
                                s3 = xrs[0] if parx == 0 else xrg
                                if prevx is not None:
                                    q3.wait_ge(vs.h, prevx)
                                s3.inc(q3.dma_start(out=xg[:, parx, :],
                                                    in_=xv_[:, do, j * CH:(j + 1) * CH]), 16)
                            w = ev7.get(id(ps))
                            if w is not None:
                                nc.tensor.wait_ge(w[0], w[1])
                            for kc in range(FH):
                                ts.inc(nc.tensor.matmul(ps[:, :], w2bv[:, 0, kc, :],
                                                        hid[:, kc, j * CH:(j + 1) * CH],
                                                        start=(kc == 0), stop=(kc == FH - 1)))
                            ts.wait(nc.vector)
                            sl = slice(j * CH, (j + 1) * CH)
                            V_.tensor_add(sA[:, do, sl], sA[:, do, sl], ps[:, :])
                            if half == 1:
                                nc.vector.wait_ge(s3.h, s3.v)
                                V_.scalar_tensor_tensor(sA[:, do, sl], xg[:, parx, :],
                                                                      sm["b2"][:, do:do + 1],
                                                                      sA[:, do, sl],
                                                                      ALU.add, ALU.add)
                                xtok[parx] = vs.v
                            ev7[id(ps)] = (vs.h, vs.v)
                        w2_mm_tok[par] = (ss.v, ts.v)
                        w2_last_mm = ts.v
                    nc.scalar.wait_ge(ts.h, ts.v)
                soft_barrier()
                for _c in (wcs[0], w1g, sds[0], w2g, zss[0], zss[1], xrs[0], xrg):
                    _c.wait(nc.sync)
            tap("tap_s1", sA[:, :, :])

        def lstm_ffn(cc_a, cc_b, tapnm, store_out=False):
            sAf = lambda c: sA[:, c, :]
            with contextlib.ExitStack() as era:
                xe = era.enter_context
                r_mu = xe(nc.sbuf_tensor(_nm(), [1, T], F32))
                r_rstd = xe(nc.sbuf_tensor(_nm(), [1, T], F32))
                mu_rstd_rows(sAf, r_mu, r_rstd, era)
                xln = xe(nc.sbuf_tensor(_nm(), [P, EC, T], BF16))
                ln_apply(sAf, lambda c: xln[:, c, :], r_mu, r_rstd, era)
                vs.wait(nc.sync)
                ld.inc(nc.sync.dma_start(out=ag1_in.ap().rearrange("(c p) t -> p c t", p=P),
                                         in_=xln[:, :, :]), 16)
                nc.gpsimd.wait_ge(ld.h, ld.v)
                cc_a.inc(nc.gpsimd.collective_compute(
                    "AllGather", ALU.bypass,
                    replica_groups=[list(range(NCORES))],
                    ins=[ag1_in.ap().opt()],
                    outs=[ag1_out.ap().opt()]))
            with contextlib.ExitStack() as era:
                xe = era.enter_context
                wres = xe(nc.sbuf_tensor(_nm(), [P, 6, EC, D], BF16))
                cT = xe(nc.sbuf_tensor(_nm(), [P, EC, TS], F32))
                cbf = xe(nc.sbuf_tensor(_nm(), [P, 2, EC, TS], BF16))
                xfb = xe(nc.sbuf_tensor(_nm(), [P, 2, EC, TS], BF16))
                g3t = xe(nc.sbuf_tensor(_nm(), [P, 2, 3, TS], F32))
                uoc = xe(nc.sbuf_tensor(_nm(), [P, EC, TS], F32))
                hst = xe(nc.sbuf_tensor(_nm(), [P, 2, TS], F32))
                hst_bf = xe(nc.sbuf_tensor(_nm(), [P, 2, TS], BF16))
                ps_g = [xe(nc.psum_tensor(_nm('ps'), [P, 2, CH], F32)) for _ in range(3)]
                lbv = lstm_bf.ap().rearrange("p (m c d) -> p m c d", m=8, c=EC)
                nc.sync.wait_ge(bgo[0].h, bgo[0].v)
                nc.sync.wait_ge(bgo[1].h, bgo[1].v)
                ld.wait(nc.sync)
                for mi in range(6):
                    ld.inc(nc.sync.dma_start(out=wres[:, mi, :, :], in_=lbv[:, mi, :, :]), 16)
                wres_tok = ld.v
                for c in range(EC):
                    V_.memset(cT[:, c, :], 0.0)
                    V_.tensor_scalar_add(cT[:, c, :], cT[:, c, :],
                                                       sm["in_c"][:, c:c + 1])
                    V_.tensor_copy(cbf[:, 0, c, :], cT[:, c, :])
                cc_a.wait(nc.sync, 1)
                ag1v = ag1_out.ap().rearrange("(r c p) t -> p r c t", p=P, c=EC)
                xfb_tok = {}
                gate_fn = [AF.Sigmoid, AF.Sigmoid, AF.Tanh]
                gate_bias = [sm["lbf"], sm["lbi"], sm["lbc"]]
                cbf_tok = vs.v
                cupd_tok = {}
                gate_tok = {}
                for b in range(NCORES):
                    prev = xfb_tok.get(b % 2)
                    if prev is not None:
                        nc.sync.wait_ge(ts.h, prev)
                    xfs[b % 2].inc(nc.sync.dma_start(out=xfb[:, b % 2, :, :],
                                                     in_=ag1v[:, b, :, bass.ds(off_ts, TS)]), 16)
                    for co in range(EC):
                        cpar = co % 2
                        for g3 in range(3):
                            ps = ps_g[g3].ap()[:, cpar, 0:TS]
                            nc.tensor.wait_ge(ld.h, wres_tok)
                            nc.tensor.wait_ge(xfs[b % 2].h, xfs[b % 2].v)
                            w = gate_tok.get((g3, cpar))
                            if w is not None:
                                nc.tensor.wait_ge(ss.h, w)
                            nc.tensor.wait_ge(vs.h, cbf_tok)
                            for ci in range(EC):
                                ts.inc(nc.tensor.matmul(ps,
                                                        wres[:, g3, ci, co * P:(co + 1) * P],
                                                        xfb[:, b % 2, ci, :],
                                                        start=(ci == 0), stop=False))
                            for ci in range(EC):
                                ts.inc(nc.tensor.matmul(ps,
                                                        wres[:, 3 + g3, ci, co * P:(co + 1) * P],
                                                        cbf[:, b % 2, ci, :],
                                                        start=False, stop=(ci == EC - 1)))
                            ts.wait(nc.scalar)
                            w2 = cupd_tok.get(cpar)
                            if w2 is not None:
                                nc.scalar.wait_ge(vs.h, w2)
                            S_.activation(g3t[:, cpar, g3, :], ps,
                                                        gate_fn[g3], bias=gate_bias[g3][:, co:co + 1])
                            gate_tok[(g3, cpar)] = ss.v
                        ss.wait(nc.vector)
                        V_.tensor_mul(g3t[:, cpar, 2, :], g3t[:, cpar, 1, :], g3t[:, cpar, 2, :])
                        V_.tensor_mul(cT[:, co, :], g3t[:, cpar, 0, :], cT[:, co, :])
                        V_.tensor_add(cT[:, co, :], cT[:, co, :], g3t[:, cpar, 2, :])
                        V_.tensor_copy(cbf[:, (b + 1) % 2, co, :], cT[:, co, :])
                        cupd_tok[cpar] = vs.v
                    cbf_tok = vs.v
                    xfb_tok[b % 2] = ts.v
                nc.sync.wait_ge(ts.h, ts.v)
                for mi06 in range(2):
                    wos.inc(nc.sync.dma_start(out=wres[:, mi06, :, :], in_=lbv[:, 6 + mi06, :, :]), 16)
                for co in range(EC):
                    wos.wait(nc.tensor)
                    nc.tensor.wait_ge(vs.h, vs.v)
                    for ci in range(EC):
                        ts.inc(nc.tensor.matmul(ps_g[0].ap()[:, co % 2, 0:TS], wres[:, 1, ci, co * P:(co + 1) * P],
                                                cbf[:, 0, ci, :], start=(ci == 0), stop=(ci == EC - 1)))
                    ts.wait(nc.vector)
                    V_.tensor_copy(uoc[:, co, :], ps_g[0].ap()[:, co % 2, 0:TS])
                ag2v = ag2_in.ap().rearrange("(c p) (b t) -> p c b t", p=P, t=TS)
                h_evtok = {}
                for b in range(NCORES):
                    prev = xfb_tok.get(b % 2)
                    if prev is not None:
                        nc.sync.wait_ge(ts.h, prev)
                    xfs[b % 2].inc(nc.sync.dma_start(out=xfb[:, b % 2, :, :],
                                                     in_=ag1v[:, b, :, bass.ds(off_ts, TS)]), 16)
                    for co in range(EC):
                        hq = (b * EC + co) % 4
                        ps = ps_g[1 + hq // 2].ap()[:, hq % 2, 0:TS]
                        nc.tensor.wait_ge(xfs[b % 2].h, xfs[b % 2].v)
                        w = h_evtok.get(hq)
                        if w is not None:
                            nc.tensor.wait_ge(vs.h, w)
                        for ci in range(EC):
                            ts.inc(nc.tensor.matmul(ps, wres[:, 0, ci, co * P:(co + 1) * P],
                                                    xfb[:, b % 2, ci, :],
                                                    start=(ci == 0), stop=(ci == EC - 1)))
                        ts.wait(nc.vector)
                        hpar = (b * EC + co) % 2
                        nc.vector.wait_ge(hss[hpar].h, hss[hpar].v)
                        V_.tensor_add(hst[:, hpar, :], ps, uoc[:, co, :])
                        h_evtok[hq] = vs.v
                        vs.wait(nc.scalar)
                        S_.activation(hst_bf[:, hpar, :], hst[:, hpar, :], AF.Tanh,
                                                    bias=sm["lbo"][:, co:co + 1])
                        ss.wait(nc.sync)
                        hss[hpar].inc(nc.sync.dma_start(out=ag2v[:, co, b, :], in_=hst_bf[:, hpar, :]), 16)
                    xfb_tok[b % 2] = ts.v
                ld.wait(nc.gpsimd)
                nc.gpsimd.wait_ge(hss[0].h, hss[0].v)
                nc.gpsimd.wait_ge(hss[1].h, hss[1].v)
                cc_b.inc(nc.gpsimd.collective_compute(
                    "AllGather", ALU.bypass,
                    replica_groups=[list(range(NCORES))],
                    ins=[ag2_in.ap().opt()],
                    outs=[ag2_out.ap().opt()]))
            with contextlib.ExitStack() as era:
                hg = era.enter_context(nc.sbuf_tensor(_nm(), [P, 2, EC, TS], BF16))
                cc_b.wait(nc.sync, 1)
                ag2o = ag2_out.ap().rearrange("(r c p) bt -> p r c bt", p=P, c=EC)
                htok = {}
                for r in range(NCORES):
                    prev = htok.get(r % 2)
                    if prev is not None:
                        nc.sync.wait_ge(vs.h, prev)
                    xrs[r % 2].inc(nc.sync.dma_start(out=hg[:, r % 2, :, :],
                                                     in_=ag2o[:, r, :, bass.ds(off_ts, TS)]), 16)
                    nc.vector.wait_ge(xrs[r % 2].h, xrs[r % 2].v)
                    V_.tensor_add(sA[:, :, r * TS:(r + 1) * TS],
                                                sA[:, :, r * TS:(r + 1) * TS],
                                                hg[:, r % 2, :, :])
                    htok[r % 2] = vs.v
                    if store_out:
                        nc.sync.wait_ge(vs.h, htok[r % 2])
                        nc.sync.wait_ge(sds[r % 2].h, sds[r % 2].v)
                        sds[r % 2].inc(nc.sync.dma_start(
                            out=out_t.rearrange("(c p) t -> p c t", p=P)[:, :, r * TS:(r + 1) * TS],
                            in_=sA[:, :, r * TS:(r + 1) * TS]), 16)
                soft_barrier()
            tap(tapnm, sA[:, :, :])

        if STAGE >= 9:
            lstm_ffn(ccs[0], ccs[1], "tap_s2")

        if STAGE >= 11:
            with contextlib.ExitStack() as era:
                xe = era.enter_context
                sAf = lambda c: sA[:, c, :]
                xln = xe(nc.sbuf_tensor(_nm(), [P, EC, T], BF16))
                with contextlib.ExitStack() as ln_era:
                    r_mu = ln_era.enter_context(nc.sbuf_tensor(_nm(), [1, T], F32))
                    r_rstd = ln_era.enter_context(nc.sbuf_tensor(_nm(), [1, T], F32))
                    mu_rstd_rows(sAf, r_mu, r_rstd, ln_era)
                    ln_apply(sAf, lambda c: xln[:, c, :], r_mu, r_rstd, ln_era)
                    soft_barrier()
                vx = xe(nc.sbuf_tensor(_nm(), [P, 16, H + 1, HD + 1], BF16))
                vxf = vx.ap().rearrange("p s h e -> p (s h e)")
                wsb = xe(nc.sbuf_tensor(_nm(), [P, EC, D], BF16))
                wvb_bf = xe(nc.sbuf_tensor(_nm(), [1, D], BF16))
                with contextlib.ExitStack() as wvbe:
                    wvb_f = wvbe.enter_context(nc.sbuf_tensor(_nm(), [1, D], F32))
                    ld.inc(nc.sync.dma_start(out=wvb_f[:, :], in_=wv_b_row), 16)
                    ld.wait(nc.vector)
                    V_.tensor_copy(wvb_bf[:, :], wvb_f[:, :])
                    vs.wait(nc.sync)
                V_.memset(vx[:, :, :, HD:HD + 1], 1.0)
                V_.memset(vx[:, :, H, :], 0.0)
                ps_qk = xe(nc.psum_tensor(_nm('ps'), [P, 4, CH], F32))
                ps_o = xe(nc.psum_tensor(_nm('ps'), [P, T], F32))
                psl = [ps_qk.ap()[:, _i, :] for _i in range(4)]
                with contextlib.ExitStack() as cera:
                    ctoks = []
                    stream_cast(lambda c: w_wv.rearrange("p (c f) -> p c f", f=D)[:, c, :],
                                wsb, EC, cera, tokens_out=ctoks, split=2)
                    evtok = {}
                    for tt in range(16):
                        for fch in range(2):
                            pi = (tt * 2 + fch) % 4
                            ps = psl[pi]
                            nc.tensor.wait_ge(ss.h, ctoks[-1])
                            w = evtok.get(pi)
                            if w is not None:
                                nc.tensor.wait_ge(vs.h, w)
                            for c in range(EC):
                                ts.inc(nc.tensor.matmul(ps[:, :], xln[:, c, tt * P:(tt + 1) * P],
                                                        wsb[:, c, fch * CH:(fch + 1) * CH],
                                                        start=(c == 0), stop=False))
                            ts.inc(nc.tensor.matmul(ps[:, :], ones_row_bf[:, :],
                                                    wvb_bf[:, fch * CH:(fch + 1) * CH],
                                                    start=False, stop=True))
                            ts.wait(nc.vector)
                            psv = ps.rearrange("p (h d) -> p h d", d=HD)
                            V_.tensor_copy(vx[:, tt, fch * 8:(fch + 1) * 8, 0:HD], psv)
                            evtok[pi] = vs.v
                watb = xe(nc.sbuf_tensor(_nm(), [P, H, 2 * HD], BF16))
                with contextlib.ExitStack() as wat_era:
                    watf = wat_era.enter_context(nc.sbuf_tensor(_nm(), [HD, H, HD], F32))
                    ld.inc(nc.sync.dma_start(out=watf[:, :, :],
                                             in_=w_att.rearrange("d (h e) -> d h e", h=H)), 16)
                    ld.wait(nc.vector)
                    wv4 = watb.ap().rearrange("p h (q e) -> p h q e", q=2)
                    for _hb in (0, HD):
                        for _q in (0, 1):
                            V_.tensor_copy(wv4[_hb:_hb + HD, :, _q, :], watf[:, :, :])
                    vs.wait(nc.tensor)
                    vs.wait(nc.sync)
                    soft_barrier()
                xwb = xe(nc.sbuf_tensor(_nm(), [P, T], BF16))
                xh2 = xe(nc.sbuf_tensor(_nm(), [P, T], BF16))
                attb = xe(nc.sbuf_tensor(_nm(), [P, 4, CH], BF16))
                osb = xe(nc.sbuf_tensor(_nm(), [HD, CH], F32))
                rrow = xe(nc.sbuf_tensor(_nm(), [1, CH], F32))
                ocat = xe(nc.sbuf_tensor(_nm(), [P, EC, T], BF16))
                exp_tok = {}
                attb_tok = {}
                qk_tok = {}
                for h in range(H):
                    hb = (h % 2) * HD
                    hc = h // 2
                    xh = xln.ap()[hb:hb + HD, hc, :]
                    for j in range(NCH):
                        pi = j % 4
                        w = evtok.get(pi)
                        if w is not None:
                            nc.tensor.wait_ge(vs.h, w)
                        ww = exp_tok.get(pi - pi % 2)
                        if ww is not None:
                            nc.tensor.wait_ge(ss.h, ww)
                        ts.inc(nc.tensor.matmul(psl[pi][:, :], watb[hb:hb + HD, h, :],
                                                xh[:, j * CH:(j + 1) * CH],
                                                start=True, stop=True))
                        ts.wait(nc.vector)
                        V_.tensor_copy(xwb[:, j * CH:(j + 1) * CH], psl[pi][:, :])
                        evtok[pi] = vs.v
                    V_.tensor_copy(xh2[0:HD, :], xh)
                    V_.tensor_copy(xh2[HD:P, :], xh)
                    vtok_xw = vs.v

                    def emit_av2(p0, qc0, st_):
                        off = (st_ * (H + 1) + h) * (HD + 1)
                        nc.tensor.wait_ge(ss.h, exp_tok[p0])
                        for d in range(2):
                            ts.inc(nc.tensor.matmul(
                                ps_o[:, (qc0 + d) * CH:(qc0 + d + 1) * CH],
                                vxf[:, off:off + P], attb[:, p0 + d, :],
                                start=(st_ == 0), stop=(st_ == 15)))
                        attb_tok[p0] = ts.v

                    pq = []
                    for st in range(16):
                        for qch in range(2):
                            qc0 = 2 * qch
                            p0 = (st * NCH + qc0) % 4
                            prev = exp_tok.get(p0)
                            if prev is not None:
                                nc.tensor.wait_ge(ss.h, prev)
                            nc.tensor.wait_ge(vs.h, vtok_xw)
                            for d in range(2):
                                wv2 = evtok.get(p0 + d)
                                if wv2 is not None:
                                    nc.tensor.wait_ge(vs.h, wv2)
                            for d in range(2):
                                ts.inc(nc.tensor.matmul(psl[p0 + d],
                                                        xh2[:, st * P:(st + 1) * P],
                                                        xwb[:, (qc0 + d) * CH:(qc0 + d + 1) * CH],
                                                        start=True, stop=True))
                            qk_tok[p0] = ts.v
                            if len(pq) == 2:
                                emit_av2(*pq.pop(0))
                            aprev = attb_tok.get(p0)
                            nc.scalar.wait_ge(ts.h, qk_tok[p0])
                            if aprev is not None:
                                nc.scalar.wait_ge(ts.h, aprev)
                            S_.activation(attb[:, p0:p0 + 2, :],
                                                        ps_qk.ap()[:, p0:p0 + 2, :],
                                                        AF.Exp, scale=0.0625)
                            exp_tok[p0] = ss.v
                            pq.append((p0, qc0, st))
                    for a in pq:
                        emit_av2(*a)
                    ts.wait(nc.vector)
                    for qc in range(NCH):
                        nc.scalar.wait_ge(ts.h, ts.v)
                        if qc == 0:
                            nc.scalar.wait_ge(vs.h, vs.v)
                        S_.activation(rrow[:, :],
                                      ps_o.ap()[HD:HD + 1, qc * CH:(qc + 1) * CH], AF.Ln)
                        nc.scalar.wait_ge(ss.h, ss.v)
                        S_.activation(rrow[:, :], rrow[:, :], AF.Exp, scale=-1.0)
                        inv_tok = ss.v
                        nc.tensor.wait_ge(ss.h, inv_tok)
                        we = exp_tok.get(0)
                        if we is not None:
                            nc.tensor.wait_ge(ss.h, we)
                        nc.tensor.wait_ge(vs.h, vs.v)
                        ts.inc(nc.tensor.matmul(psl[0][0:HD, :], ones_row[:, 0:HD],
                                                rrow[:, :],
                                                start=True, stop=True))
                        ts.wait(nc.vector)
                        V_.tensor_copy(osb[:, :],
                                                     ps_o.ap()[0:HD, qc * CH:(qc + 1) * CH])
                        V_.tensor_mul(
                            ocat.ap()[hb:hb + HD, hc, qc * CH:(qc + 1) * CH],
                            osb[:, :], psl[0][0:HD, :])
                        evtok[0] = vs.v
                with contextlib.ExitStack() as cera:
                    ctoks = []
                    stream_cast(lambda c: w_outw.rearrange("p (c f) -> p c f", f=D)[:, c, :],
                                wsb, EC, cera, tokens_out=ctoks, split=2)
                    for do in range(EC):
                        for j in range(NCH):
                            pi = (do * NCH + j) % 4
                            nc.tensor.wait_ge(ss.h, ctoks[-1])
                            we = exp_tok.get(pi - pi % 2)
                            if we is not None:
                                nc.tensor.wait_ge(ss.h, we)
                            w = evtok.get(pi)
                            if w is not None:
                                nc.tensor.wait_ge(vs.h, w)
                            for c in range(EC):
                                ts.inc(nc.tensor.matmul(psl[pi], wsb[:, c, do * P:(do + 1) * P],
                                                        ocat[:, c, j * CH:(j + 1) * CH],
                                                        start=(c == 0), stop=(c == EC - 1)))
                            ts.wait(nc.vector)
                            sl = slice(j * CH, (j + 1) * CH)
                            V_.scalar_tensor_tensor(sA[:, do, sl], psl[pi],
                                                                  sm["out_b"][:, do:do + 1],
                                                                  sA[:, do, sl],
                                                                  ALU.add, ALU.add)
                            evtok[pi] = vs.v
                soft_barrier()
            tap("tap_s3", sA[:, :, :])

        if STAGE >= 12:
            lstm_ffn(ccs[2], ccs[3], "tap_s4", store_out=True)
        else:
            ld.inc(nc.sync.dma_start(out=out_t.rearrange("(c p) t -> p c t", p=P),
                                     in_=sA[:, :, :]), 16)

        nc.all_engine_barrier()
        for _ctr in [ld, gd, vs, ss, ts, gs] + bgi + bgo + zss + [zcs] + hss + wcs + sds + xfs + xrs + [wos, w1g, w2g, xrg]:
            _ctr.wait(nc.sync)
        ld.wait(nc.sync)
        ld.wait(nc.gpsimd)
        sA_g.__exit__(None, None, None)
        off_reg_g.__exit__(None, None, None)
        rank_reg_g.__exit__(None, None, None)
    return nc, tap_ap


def _vec128(v):
    v = np.asarray(v, dtype=np.float32)
    return np.ascontiguousarray(v.reshape(-1, P).T)


def _mat128(w):
    w = np.asarray(w, dtype=np.float32)
    k, n = w.shape
    return np.ascontiguousarray(
        w.reshape(k // P, P, n).transpose(1, 0, 2).reshape(P, (k // P) * n))


def _host_inputs(inputs):
    g = lambda n: np.asarray(inputs[n], dtype=np.float32)
    sm = {
        "ln1_g": g("L0_attn_a_ln1_g"), "ln1_b": g("L0_attn_a_ln1_b"),
        "ln2_g": g("L0_attn_a_ln2_g"), "ln2_b": g("L0_attn_a_ln2_b"),
        "rms_w": g("L0_attn_a_rms_w"), "conv_b": g("L0_attn_a_conv_b"),
        "dt_b": g("L0_attn_a_dt_proj_b"), "d_par": g("L0_attn_a_D_param"),
        "b2": g("L0_attn_a_ffn_b2"),
        "lbf": g("L0_ffn_bf"), "lbi": g("L0_ffn_bi"), "lbc": g("L0_ffn_bc"),
        "lbo": g("L0_ffn_bo"), "in_c": g("L0_ffn_in_c"),
        "out_b": g("L0_attn_b_out_b"),
    }
    shared = {("sm_" + n): _vec128(v) for n, v in sm.items()}
    shared["sm_b1"] = _vec128(g("L0_attn_a_ffn_b1"))
    shared["wv_b_row"] = np.ascontiguousarray(g("L0_attn_b_wv_b").reshape(1, D))
    sel = np.zeros((32, 32, P), dtype=np.float32)
    for n in range(32):
        sel[n, n, :] = 1.0
    shared["sel_in"] = np.ascontiguousarray(sel.reshape(32, 32 * P))
    shared["conv_w_in"] = np.ascontiguousarray(
        g("L0_attn_a_conv_w").reshape(EC, P, 4).transpose(1, 0, 2).reshape(P, EC * 4))
    shared["a_log_in"] = np.ascontiguousarray(
        g("L0_attn_a_A_log").reshape(EC, P, NST).transpose(1, 0, 2).reshape(P, EC * NST))
    shared["w_in"] = _mat128(g("L0_attn_a_in_proj_w"))
    shared["w_xproj"] = _mat128(g("L0_attn_a_x_proj_w"))
    shared["w_dt"] = np.ascontiguousarray(g("L0_attn_a_dt_proj_w"))
    shared["w_outp"] = _mat128(g("L0_attn_a_out_proj_w"))
    w1 = g("L0_attn_a_ffn_w1")
    shared["w_ffn1"] = np.ascontiguousarray(
        w1.reshape(EC, P, FC, P).transpose(1, 2, 0, 3).reshape(P, FC * EC * P))
    w2 = g("L0_attn_a_ffn_w2")
    shared["w_ffn2"] = np.ascontiguousarray(
        w2.reshape(2, FH, P, EC, P).transpose(2, 0, 3, 1, 4).reshape(P, 2 * EC * FH * P))
    shared["w_att"] = np.ascontiguousarray(
        g("L0_attn_b_w_att_val").transpose(1, 0, 2).reshape(HD, H * HD))
    shared["w_wv"] = _mat128(np.ascontiguousarray(g("L0_attn_b_wv_w").T))
    shared["w_outw"] = _mat128(np.ascontiguousarray(g("L0_attn_b_out_w").T))
    lst = [g("L0_ffn_wf").T, g("L0_ffn_wi").T, g("L0_ffn_wc").T,
           g("L0_ffn_uf").T, g("L0_ffn_ui").T, g("L0_ffn_uc").T,
           g("L0_ffn_wo").T, g("L0_ffn_uo").T]
    lw = np.stack([_mat128(np.ascontiguousarray(m)) for m in lst], axis=1)
    shared["w_lstm"] = np.ascontiguousarray(lw.reshape(P, 8 * EC * D))
    return shared


def kernel(**inputs):
    nc, _ = build(taps=DEV_TAPS)
    shared = _host_inputs(inputs)
    x = np.asarray(inputs["L0_attn_a_x"], dtype=np.float32)
    in_maps = []
    for r in range(NCORES):
        m = dict(shared)
        m["x_t"] = np.ascontiguousarray(x[r].T)
        m["rank_in"] = np.array([[r]], dtype=np.int32)
        in_maps.append(m)
    res = run_bass_kernel_spmd(nc, in_maps, core_ids=list(range(NCORES)))
    out = np.stack([np.asarray(res.results[r]["out_t"]).T for r in range(NCORES)], axis=0)
    return out.astype(np.float32)
